# version 01 (final)
# baseline (speedup 1.0000x reference)
import os
import numpy as np
import contextlib
import concourse.bass as bass
import concourse.mybir as mybir
from concourse.bass_utils import run_bass_kernel_spmd

F32 = mybir.dt.float32
F16 = mybir.dt.float16
BF16 = mybir.dt.bfloat16
I32 = mybir.dt.int32
AF = mybir.ActivationFunctionType
ALU = mybir.AluOpType
AX = mybir.AxisListType

S = 2048
D = 1024
H = 16
HD = 64
DL = 256
DI = 3072
LAM = 0.01
ND = D // 128
NDL = DL // 128
NI = DI // 128
NT = S // 128
NS = S // 512
QMAX = 32767.0
EPS = 1e-5

SC_RSX, SC_SX, SC_RSUP, SC_RSG, SC_SG, SC_KAP0, SC_RSG2, SC_KAP3, \
    SC_RSO, SC_SO, SC_T0, SC_T1 = range(12)


class Ctr:
    def __init__(self, h):
        self.h = h
        self.n = 0

    def inc(self, instr, k=1):
        instr.then_inc(self.h, k)
        self.n += k
        return instr


class _Stop(Exception):
    pass


def build(debug=0, stop=99, dump_half=0):
    nc = bass.Bass("TRN2", target_bir_lowering=False)

    def din(name, shape, dt=F32):
        return nc.dram_tensor(name, shape, dt, kind="ExternalInput").ap()

    x_in = din("x", [S, D])
    c_in = din("c", [1, D])
    wmod_in = din("wmod", [D, 6 * D])
    Wq_in = din("Wq", [D, D])
    Wk_in = din("Wk", [D, D])
    Wv_in = din("Wv", [D, D])
    Wo_in = din("Wo", [D, D])
    Wdkv_in = din("Wdkv", [D, DL])
    Wupk_in = din("Wupk", [DL, D])
    Wupv_in = din("Wupv", [DL, D])
    wg_in = din("w_gate", [DI, D])
    wu_in = din("w_up", [DI, D])
    wd_in = din("w_down", [D, DI])
    ident_in = din("identc", [128, 128])
    onesr_in = din("ones_row", [128, 128])
    onesc_in = din("ones_col", [128, 1])
    onescb_in = din("ones_col_bf", [128, 1], BF16)
    dra_in = din("dra", [1, S])
    drb_in = din("drb", [1, S])
    cmask_in = din("cmask", [128, 128])

    y = nc.dram_tensor("y", [S, D], F32, kind="ExternalOutput").ap()

    sp_x = nc.dram_tensor("sp_x", [ND, 128, S], F32).ap()
    sp_up = nc.dram_tensor("sp_up", [NI, 128, S], F16).ap()
    sp_gate = nc.dram_tensor("sp_gate", [NI, 128, S], F16).ap()
    sp_wu = nc.dram_tensor("sp_wu", [DI, D], BF16).ap()
    sp_wg = nc.dram_tensor("sp_wg", [DI, D], BF16).ap()
    sp_wd = nc.dram_tensor("sp_wd", [D, DI], BF16).ap()
    cc_i = [nc.dram_tensor(f"cc_in{k}", [1, 2], F32).ap() for k in range(4)]
    cc_o = [nc.dram_tensor(f"cc_out{k}", [1, 2], F32).ap() for k in range(4)]

    dbg = {}
    if debug:
        for nm, shape, dt in [
                ("d_xT", [128, ND * S], F32), ("d_mod", [128, 48], F32),
                ("d_h1", [128, ND * S], BF16), ("d_kv", [128, ND * S], BF16),
                ("d_lat", [128, NDL * S], BF16), ("d_qa", [128, 4 * S], BF16),
                ("d_kra", [128, 4 * S], BF16), ("d_negm", [1, 16], F32),
                ("d_attn", [128, ND * S], BF16), ("d_xmid", [128, ND * S], F32),
                ("d_xq", [128, ND * S], BF16), ("d_scal", [1, 16], F32),
                ("d_mid0", [128, S], BF16), ("d_wut", [128, DI], BF16),
                ("d_qmx", [128, 4], F32), ("d_kmx", [128, 4], F32),
                ("d_qbrow", [1, 16], F32), ("d_negall", [128, 16], F32),
                ("d_rnorm", [128, 1024], F32), ("d_spwu", [128, 1024], BF16), ("d_fs", [128, 32], F32),
                ("d_fw", [128, 512], F32), ("d_fmx", [128, 32], F32), ("d_frs", [128, 32], F32),
                ("d_fdv", [128, 512], F32),
                ("d_expt", [128, 1024], BF16), ("d_vton", [128, NT * 65], BF16),
                ("d_scal2", [1, 16], F32)]:
            dbg[nm] = nc.dram_tensor(nm, shape, dt, kind="ExternalOutput").ap()

    outer = contextlib.ExitStack()
    with outer:
      try:
        def mkstack():
            return outer.enter_context(contextlib.ExitStack())

        def SB(stk, name, shape, dt, side="left"):
            return stk.enter_context(nc.sbuf_tensor(name, shape, dt, side=side))

        def PSU(stk, name, shape):
            return stk.enter_context(nc.psum_tensor(name, shape, F32))

        def SEM(name):
            return Ctr(outer.enter_context(nc.semaphore(name)))

        identt = SB(outer, "identt", [128, 128], F32)
        onesr = SB(outer, "onesr", [128, 128], F32)
        onesc = SB(outer, "onesc", [128, 1], F32)
        onescb = SB(outer, "onescb", [128, 1], BF16)
        cmaskt = SB(outer, "cmaskt", [128, 128], F32)
        modT = SB(outer, "modT", [128, 48], F32)
        caP = SB(outer, "caP", [128, 8], F32)
        cbP = SB(outer, "cbP", [128, 8], F32)
        gaP = SB(outer, "gaP", [128, 8], F32)
        gbP = SB(outer, "gbP", [128, 8], F32)
        gbsoP = SB(outer, "gbsoP", [128, 8], F32)
        tinyP = SB(outer, "tinyP", [128, 16], F32)
        negMall = SB(outer, "negMall", [128, 16], F32)
        negMrow = SB(outer, "negMrow", [1, 16], F32)
        qkrow = SB(outer, "qkrow", [1, 32], F32)
        scal = SB(outer, "scal", [1, 16], F32)
        sclB = SB(outer, "sclB", [128, 12], F32)
        ccs = SB(outer, "ccs", [1, 8], F32)
        epsc = SB(outer, "epsc", [128, 1], F32)

        g_f0 = mkstack()
        F0W = 512
        f0_wst = SB(g_f0, "f0_wst", [128, 2 * F0W], F32)
        f0_dv = SB(g_f0, "f0_dv", [128, F0W], F32)
        f0_i32 = SB(g_f0, "f0_i32", [128, F0W], I32)
        f0_bq = SB(g_f0, "f0_bq", [128, F0W], BF16)
        f0_mx = SB(g_f0, "f0_mx", [128, 32], F32)
        f0_mn = SB(g_f0, "f0_mn", [128, 32], F32)
        f0_s = SB(g_f0, "f0_s", [128, 32], F32)
        f0_rs = f0_mn

        LD = SEM("LD")
        LS = SEM("LS")
        LG = SEM("LG")
        TP = SEM("TP")
        VP = SEM("VP")
        SPm = SEM("SPm")
        GP = SEM("GP")
        CC = SEM("CC")
        BR = SEM("BR")
        DS = SEM("DS")
        sSC = SEM("sSC")
        sDF = SEM("sDF")
        sEX = SEM("sEX")
        sAV = SEM("sAV")
        sNM1 = SEM("sNM1")
        sNM2 = SEM("sNM2")
        sNM3 = SEM("sNM3")
        sNM4 = SEM("sNM4")

        block = outer.enter_context(nc.Block())
        nbar = [0]

        def ck(n):
            if stop <= n:
                raise _Stop()

        def barrier():
            nbar[0] += 1
            tgt = 5 * nbar[0]
            for engname in ("sync", "tensor", "vector", "scalar", "gpsimd"):
                def emit(eng):
                    eng.drain().then_inc(BR.h, 1)
                    eng.wait_ge(BR.h, tgt)
                getattr(block, engname)(emit)
            BR.n = tgt

        def dbg_dump(name, ap_src, deps):
            if not debug:
                return

            def emit(g):
                for sem, val in deps:
                    g.wait_ge(sem.h, val)
                DS.inc(g.dma_start(out=dbg[name], in_=ap_src), 16)
            block.gpsimd(emit)

        def snap():
            return {s: s.n for s in (LD, LS, LG, TP, VP, SPm, GP, CC)}

        g_x1 = mkstack()
        xT = SB(g_x1, "xT", [128, ND * S], F32)
        g_a = mkstack()
        arena = SB(g_a, "arena", [128, NT * D], F32)

        b = snap()

        def _(sync):
            for dst, src in [(identt, ident_in), (onesr, onesr_in),
                             (onesc, onesc_in), (onescb, onescb_in),
                             (cmaskt, cmask_in)]:
                LD.inc(sync.dma_start(out=dst[:], in_=src), 16)
            LD.inc(sync.dma_start(
                out=arena[:].rearrange("p (n d) -> p n d", d=D),
                in_=x_in.rearrange("(n p) d -> p n d", p=128)), 16)
        block.sync(_)
        ld_x = LD.n

        g_ps0 = mkstack()
        ps_tr = PSU(g_ps0, "ps_tr", [128, 512])
        ps_tr2 = PSU(g_ps0, "ps_tr2", [128, 512])
        ps_mod = PSU(g_ps0, "ps_mod", [128, 512])

        def _(t):
            t.wait_ge(LD.h, ld_x)
            for dc in range(ND):
                for g in range(4):
                    i = dc * 4 + g
                    pst = ps_tr if i % 2 == 0 else ps_tr2
                    if i >= 2:
                        t.wait_ge(VP.h, b[VP] + i - 1)
                    for j in range(4):
                        n = g * 4 + j
                        ins = t.transpose(pst[:, j * 128:(j + 1) * 128],
                                          arena[:, n * D + dc * 128:n * D + (dc + 1) * 128],
                                          identt[:])
                    TP.inc(ins, 1)
        block.tensor(_)

        def _(v):
            v.memset(epsc[:], EPS)
            for dc in range(ND):
                for g in range(4):
                    i = dc * 4 + g
                    pst = ps_tr if i % 2 == 0 else ps_tr2
                    v.wait_ge(TP.h, b[TP] + i + 1)
                    VP.inc(v.tensor_copy(
                        xT[:, dc * S + g * 512:dc * S + (g + 1) * 512], pst[:]), 1)
        block.vector(_)
        vp_xtr = VP.n
        dbg_dump("d_xT", xT[:], [(VP, vp_xtr)])
        barrier()
        g_a.close()
        ck(1)

        g_h1 = mkstack()
        h1T = SB(g_h1, "h1T", [128, ND * S], BF16, side="right")
        g_dr = mkstack()
        DRA = SB(g_dr, "DRA", [128, S], BF16, side="right")
        DRB = SB(g_dr, "DRB", [128, S], BF16, side="right")
        g_a1 = mkstack()
        WM_H = 3 * D
        wmarena = SB(g_a1, "wmarena", [128, 2 * WM_H], F32)
        silc = SB(g_a1, "silc", [1, D], F32)
        cch = SB(g_a1, "cch", [1, D], F32)
        scT = SB(g_a1, "scT", [128, 8], F32)
        drar = SB(g_a1, "drar", [1, S], F32)
        drbr = SB(g_a1, "drbr", [1, S], F32)

        def _(sync):
            LD.inc(sync.dma_start(out=cch[:], in_=c_in), 16)
            LD.inc(sync.dma_start(out=drar[:], in_=dra_in), 16)
            LD.inc(sync.dma_start(out=drbr[:], in_=drb_in), 16)
        block.sync(_)
        ld_c = LD.n

        def _(s):
            s.wait_ge(LD.h, ld_c)
            SPm.inc(s.activation(silc[:], cch[:], AF.Silu), 1)
        block.scalar(_)
        sp_silu = SPm.n

        b2 = snap()

        def _(sync):
            for kh in range(2 * ND):
                if kh >= 2:
                    sync.wait_ge(TP.h, b2[TP] + 1 + (kh - 1) * 24)
                k, hh = kh // 2, kh % 2
                LD.inc(sync.dma_start(
                    out=wmarena[:, (kh % 2) * WM_H:(kh % 2 + 1) * WM_H],
                    in_=wmod_in[k * 128:(k + 1) * 128, hh * WM_H:(hh + 1) * WM_H]), 16)
        block.sync(_)

        def _(t):
            t.wait_ge(SPm.h, sp_silu)
            for j in range(8):
                ins = t.matmul(ps_tr[:, 384 + j:385 + j],
                               silc[0:1, j * 128:(j + 1) * 128],
                               identt[0:1, 0:1], start=True, stop=True)
            TP.inc(ins, 1)
            t.wait_ge(VP.h, b2[VP] + 1)
            for kh in range(2 * ND):
                k, hh = kh // 2, kh % 2
                t.wait_ge(LD.h, b2[LD] + (kh + 1) * 16)
                for m in range(24):
                    ins = t.matmul(ps_mod[:, hh * 24 + m:hh * 24 + m + 1],
                                   wmarena[:, (kh % 2) * WM_H + m * 128:
                                           (kh % 2) * WM_H + (m + 1) * 128],
                                   scT[:, k:k + 1],
                                   start=(kh == 0 and m == 0), stop=(k == ND - 1))
                    TP.inc(ins, 1)
            t.wait_ge(LD.h, ld_c)
            for half, row in ((0, drar), (1, drbr)):
                for n in range(NS):
                    j = half * NS + n
                    if j >= 2:
                        t.wait_ge(VP.h, b2[VP] + j + 3)
                    pst = ps_tr2 if j % 2 == 0 else ps_tr
                    ins = t.matmul(pst[:, 0:512], onesr[0:1, :],
                                   row[0:1, n * 512:(n + 1) * 512], start=True, stop=True)
                    TP.inc(ins, 1)
        block.tensor(_)
        tp_sct = b2[TP] + 1
        tp_mod = tp_sct + 24 * 2 * ND
        tp_dr0 = tp_mod

        def _(v):
            v.wait_ge(TP.h, tp_sct)
            v.tensor_copy(scT[:], ps_tr[:, 384:392])
            VP.inc(v.drain(), 1)
            v.wait_ge(TP.h, tp_mod)
            v.tensor_copy(modT[:], ps_mod[:, 0:48])
            VP.inc(v.drain(), 1)
            v.tensor_scalar_add(caP[:], modT[:, 8:16], 1.0)
            v.tensor_scalar_add(cbP[:], modT[:, 32:40], 1.0)
            for (gsrc, gdst) in ((modT[:, 16:24], gaP), (modT[:, 40:48], gbP)):
                v.tensor_scalar(tinyP[:, 0:8], gsrc, 0.0, None, ALU.is_ge)
                v.drain()
                v.tensor_scalar(tinyP[:, 8:16], tinyP[:, 0:8], 2.0, -1.0,
                                ALU.mult, ALU.add)
                v.drain()
                v.tensor_tensor(tinyP[:, 0:8], gsrc, tinyP[:, 8:16], ALU.mult)
                v.drain()
                v.tensor_scalar(gdst[:], tinyP[:, 0:8], 0.25, None, ALU.max)
                v.drain()
                v.tensor_tensor(gdst[:], gdst[:], tinyP[:, 8:16], ALU.mult)
                VP.inc(v.drain(), 1)
            for half, dst in ((0, DRA), (1, DRB)):
                for n in range(NS):
                    j = half * NS + n
                    v.wait_ge(TP.h, tp_dr0 + j + 1)
                    pst = ps_tr2 if j % 2 == 0 else ps_tr
                    VP.inc(v.tensor_copy(dst[:, n * 512:(n + 1) * 512], pst[:, 0:512]), 1)
        block.vector(_)
        vp_mod = VP.n
        dbg_dump("d_mod", modT[:], [(VP, vp_mod)])
        barrier()
        g_a1.close()
        ck(2)

        g_wb = mkstack()
        WqBF = SB(g_wb, "WqBF", [128, ND * D], BF16, side="right")
        WupkBF = SB(g_wb, "WupkBF", [128, NDL * D], BF16, side="right")
        WupvBF = SB(g_wb, "WupvBF", [128, NDL * D], BF16, side="right")
        g_lat = mkstack()
        latentT = SB(g_lat, "latentT", [128, NDL * S], BF16, side="right")
        g_wkv = mkstack()
        WkvBF = SB(g_wkv, "WkvBF", [128, ND * D], BF16, side="right")
        WdkvBF = SB(g_wkv, "WdkvBF", [128, ND * DL], BF16, side="right")
        g_a2 = mkstack()
        stg = SB(g_a2, "stg", [128, 2 * D], F32)

        b = snap()

        def _(sync):
            for k in range(ND):
                if k >= 2:
                    sync.wait_ge(VP.h, b[VP] + k - 1)
                LD.inc(sync.dma_start(out=stg[:, (k % 2) * D:(k % 2 + 1) * D],
                                      in_=Wq_in[k * 128:(k + 1) * 128, :]), 16)
            for k in range(ND):
                sync.wait_ge(VP.h, b[VP] + ND + k)
                LD.inc(sync.dma_start(out=stg[:, 0:D],
                                      in_=Wk_in[k * 128:(k + 1) * 128, :]), 16)
                LD.inc(sync.dma_start(out=stg[:, D:2 * D],
                                      in_=Wv_in[k * 128:(k + 1) * 128, :]), 16)
            sync.wait_ge(VP.h, b[VP] + 2 * ND)
            LD.inc(sync.dma_start(
                out=stg[:, 0:ND * DL].rearrange("p (n d) -> p n d", d=DL),
                in_=Wdkv_in.rearrange("(n p) d -> p n d", p=128)), 16)
            sync.wait_ge(VP.h, b[VP] + 2 * ND + 1)
            LD.inc(sync.dma_start(
                out=stg[:, 0:NDL * D].rearrange("p (n d) -> p n d", d=D),
                in_=Wupk_in.rearrange("(n p) d -> p n d", p=128)), 16)
            sync.wait_ge(VP.h, b[VP] + 2 * ND + 2)
            LD.inc(sync.dma_start(
                out=stg[:, 0:NDL * D].rearrange("p (n d) -> p n d", d=D),
                in_=Wupv_in.rearrange("(n p) d -> p n d", p=128)), 16)
        block.sync(_)

        def _(v):
            for k in range(ND):
                v.wait_ge(LD.h, b[LD] + (k + 1) * 16)
                VP.inc(v.tensor_scalar_mul(WqBF[:, k * D:(k + 1) * D],
                                           stg[:, (k % 2) * D:(k % 2 + 1) * D],
                                           1.0 / 32.0), 1)
            for k in range(ND):
                v.wait_ge(LD.h, b[LD] + ND * 16 + (k + 1) * 32)
                v.tensor_tensor(stg[:, 0:D], stg[:, 0:D], stg[:, D:2 * D], ALU.add)
                VP.inc(v.tensor_copy(WkvBF[:, k * D:(k + 1) * D], stg[:, 0:D]), 1)
            v.wait_ge(LD.h, b[LD] + 3 * ND * 16 + 16)
            VP.inc(v.tensor_copy(WdkvBF[:], stg[:, 0:ND * DL]), 1)
            v.wait_ge(LD.h, b[LD] + 3 * ND * 16 + 32)
            VP.inc(v.tensor_copy(WupkBF[:], stg[:, 0:NDL * D]), 1)
            v.wait_ge(LD.h, b[LD] + 3 * ND * 16 + 48)
            VP.inc(v.tensor_copy(WupvBF[:], stg[:, 0:NDL * D]), 1)
        block.vector(_)
        barrier()
        g_a2.close()
        ck(3)
        g_ps0.close()

        f0_pieces = []
        for (w_src, w_dst, nch, wid) in ((wu_in, sp_wu, NI, D), (wg_in, sp_wg, NI, D)):
            for i in range(nch):
                for c0 in range(0, wid, F0W):
                    c1 = min(wid, c0 + F0W)
                    f0_pieces.append((w_src, w_dst, i, c0, c1))
        for i in range(ND):
            for c0 in range(0, DI, F0W):
                f0_pieces.append((wd_in, sp_wd, i, c0, c0 + F0W))
        bf0 = snap()
        NP0 = len(f0_pieces)

        def _(g):
            for p, (w_src, w_dst, i, c0, c1) in enumerate(f0_pieces):
                wd = c1 - c0
                par = p % 2
                if p >= 2:
                    g.wait_ge(GP.h, bf0[GP] + p - 1)
                LG.inc(g.dma_start(out=f0_wst[:, par * F0W:par * F0W + wd],
                                   in_=w_src[i * 128:(i + 1) * 128, c0:c1]), 16)
                g.wait_ge(VP.h, bf0[VP] + 4 * (p + 1))
                if debug and p == 4:
                    DS.inc(g.dma_start(out=dbg["d_fs"], in_=f0_s[:]), 16)
                    DS.inc(g.dma_start(out=dbg["d_fw"],
                                       in_=f0_wst[:, 0:F0W]), 16)
                    DS.inc(g.dma_start(out=dbg["d_fmx"], in_=f0_mx[:]), 16)
                nb = wd // 32
                sb = f0_s[:, par * 16:par * 16 + nb].rearrange(
                    "p a -> p a ()").broadcast_to([128, nb, 32])
                rsb = f0_rs[:, par * 16:par * 16 + nb].rearrange(
                    "p a -> p a ()").broadcast_to([128, nb, 32])
                wv = f0_wst[:, par * F0W:par * F0W + wd].rearrange(
                    "p (a c) -> p a c", c=32)
                g.tensor_tensor(
                    f0_dv[:, 0:wd].rearrange("p (a c) -> p a c", c=32),
                    wv, rsb, ALU.mult)
                g.tensor_copy(f0_i32[:, 0:wd], f0_dv[:, 0:wd])
                g.tensor_copy(f0_dv[:, 0:wd], f0_i32[:, 0:wd])
                g.tensor_tensor(
                    f0_wst[:, par * F0W:par * F0W + wd].rearrange(
                        "p (a c) -> p a c", c=32),
                    f0_dv[:, 0:wd].rearrange("p (a c) -> p a c", c=32),
                    sb, ALU.mult)
                g.tensor_copy(f0_bq[:, 0:wd],
                              f0_wst[:, par * F0W:par * F0W + wd])
                GP.inc(g.drain(), 1)
                g.wait_ge(GP.h, GP.n)
                if debug and p == 4:
                    DS.inc(g.dma_start(out=dbg["d_frs"], in_=f0_rs[:]), 16)
                    DS.inc(g.dma_start(out=dbg["d_fdv"], in_=f0_dv[:]), 16)
                LG.inc(g.dma_start(out=w_dst[i * 128:(i + 1) * 128, c0:c1],
                                   in_=f0_bq[:, 0:wd]), 16)
        block.gpsimd(_)

        def _(v):
            for p, (w_src, w_dst, i, c0, c1) in enumerate(f0_pieces):
                wd = c1 - c0
                nb = wd // 32
                par = p % 2
                v.wait_ge(LG.h, bf0[LG] + p * 32 + 16)
                wv = f0_wst[:, par * F0W:par * F0W + wd].rearrange(
                    "p (a c) -> p a c", c=32)
                mx = f0_mx[:, par * 16:par * 16 + nb]
                mn = f0_mn[:, par * 16:par * 16 + nb]
                sc = f0_s[:, par * 16:par * 16 + nb]
                v.tensor_reduce(mx, wv, AX.X, ALU.max, apply_absolute_value=True)
                VP.inc(v.drain(), 2)
                v.tensor_scalar(sc, mx, 7e-12, 1.0 / 7.0, ALU.max, ALU.mult)
                v.drain()
                v.reciprocal(f0_rs[:, par * 16:par * 16 + nb], sc)
                VP.inc(v.drain(), 2)
        block.vector(_)
        lg_f0 = LG.n
        ck(4)

        def emit_layernorm(tag, xbuf, cP, saCols, dst, absmax_cols=None):
            g_ln = mkstack()
            rowbuf = SB(g_ln, f"rows{tag}", [128, S], F32)
            murow = rowbuf[0:1, :]
            rrow = rowbuf[32:33, :]
            brow = rowbuf[64:65, :]
            sq0 = SB(g_ln, f"sq{tag}", [128, S], BF16)
            sq = [sq0, sq0]
            ub = SB(g_ln, f"ub{tag}", [128, S], F32)
            Rb = SB(g_ln, f"Rb{tag}", [128, S], F32)
            Bb = SB(g_ln, f"Bb{tag}", [128, S], F32)
            ps1 = PSU(g_ln, f"ps1{tag}", [128, S])
            ps2 = PSU(g_ln, f"ps2{tag}", [128, S])
            bb = snap()

            def _(s):
                for c in range(ND):
                    if c >= 1:
                        s.wait_ge(TP.h, bb[TP] + c * NS)
                    SPm.inc(s.activation(sq[c % 2][:], xbuf[:, c * S:(c + 1) * S],
                                         AF.Square), 1)
            block.scalar(_)

            def _(t):
                for c in range(ND):
                    for n in range(NS):
                        t.matmul(ps1[0:1, n * 512:(n + 1) * 512], onesc[:],
                                 xbuf[:, c * S + n * 512:c * S + (n + 1) * 512],
                                 start=(c == 0), stop=(c == ND - 1))
                    t.wait_ge(SPm.h, bb[SPm] + c + 1)
                    for n in range(NS):
                        ins = t.matmul(ps2[0:1, n * 512:(n + 1) * 512], onescb[:],
                                       sq[c % 2][:, n * 512:(n + 1) * 512],
                                       start=(c == 0), stop=(c == ND - 1))
                        TP.inc(ins, 1)
                t.wait_ge(VP.h, bb[VP] + 4)
                t.wait_ge(SPm.h, bb[SPm] + ND + 2)
                for n in range(NS):
                    t.matmul(ps2[:, n * 512:(n + 1) * 512], onesr[0:1, :],
                             Rb[0:1, n * 512:(n + 1) * 512],
                             start=True, stop=True)
                    ins = t.matmul(ps1[:, n * 512:(n + 1) * 512], onesr[64:65, :],
                                   rowbuf[64:65, n * 512:(n + 1) * 512],
                                   start=True, stop=True)
                    TP.inc(ins, 1)
            block.tensor(_)
            tp_stats = bb[TP] + ND * NS
            tp_bc = tp_stats + NS

            def _(v):
                v.wait_ge(TP.h, tp_stats)
                v.tensor_scalar_mul(murow, ps1[0:1, :], 1.0 / D)
                v.tensor_tensor(Rb[0:1, :], murow, murow, ALU.mult)
                v.tensor_scalar_mul(Bb[0:1, :], ps2[0:1, :], 1.0 / D)
                VP.inc(v.tensor_tensor(rrow, Bb[0:1, :], Rb[0:1, :],
                                       ALU.subtract), 1)
                v.wait_ge(SPm.h, bb[SPm] + ND + 2)
                v.tensor_scalar_mul(Bb[0:1, :], murow, -1.0)
                VP.inc(v.tensor_tensor(brow, Bb[0:1, :], Rb[0:1, :],
                                       ALU.mult), 3)
                v.wait_ge(TP.h, tp_bc)
                v.tensor_copy(Rb[:], ps2[:])
                VP.inc(v.tensor_copy(Bb[:], ps1[:]), 1)
                for c in range(ND):
                    if c >= 1:
                        v.wait_ge(SPm.h, bb[SPm] + ND + 2 + c)
                    v.tensor_tensor(ub[:], xbuf[:, c * S:(c + 1) * S], Rb[:],
                                    ALU.mult)
                    VP.inc(v.tensor_tensor(ub[:], ub[:], Bb[:], ALU.add), 1)
            block.vector(_)
            vp_rows = bb[VP] + 1
            vp_bc = bb[VP] + 5

            def _(s):
                s.wait_ge(VP.h, vp_rows)
                s.activation(brow, rrow, AF.Ln, bias=epsc[64:65, :])
                SPm.inc(s.activation(Rb[0:1, :], brow, AF.Exp, scale=-0.5), 2)
                for c in range(ND):
                    s.wait_ge(VP.h, vp_bc + c + 1)
                    SPm.inc(s.activation(dst[:, c * S:(c + 1) * S], ub[:],
                                         AF.Identity, bias=saCols[:, c:c + 1],
                                         scale=cP[:, c:c + 1]), 1)
            block.scalar(_)
            sp_apply0 = bb[SPm] + ND + 2

            if absmax_cols is not None:
                def _(v):
                    for c in range(ND):
                        v.wait_ge(SPm.h, sp_apply0 + c + 1)
                        VP.inc(v.tensor_reduce(absmax_cols[:, c:c + 1],
                                               dst[:, c * S:(c + 1) * S], AX.X,
                                               ALU.max, apply_absolute_value=True), 1)
                block.vector(_)
            g_ln.close()

        emit_layernorm("L1", xT, caP, modT[:, 0:8], h1T)
        sp_b0 = SPm.n
        dbg_dump("d_h1", h1T[:], [(SPm, sp_b0)])

        def _(s):
            s.wait_ge(SPm.h, sp_b0)
            for c in range(ND):
                LS.inc(s.dma_start(out=sp_x[c], in_=xT[:, c * S:(c + 1) * S]), 16)
        block.scalar(_)
        ls_xspill = LS.n
        barrier()
        g_x1.close()
        ck(5)

        g_kv = mkstack()
        kvT = SB(g_kv, "kvT", [128, ND * S], BF16)
        g_c0p = mkstack()
        psA = PSU(g_c0p, "psA", [128, 512])
        psB = PSU(g_c0p, "psB", [128, 512])

        def emit_proj(psA, psB, wbf, wcols, src, srcchunks, nout, dst_write):
            bb = snap()

            def _(t):
                i = 0
                for c in range(nout):
                    for n in range(NS):
                        ps = psA if i % 2 == 0 else psB
                        t.wait_ge(VP.h, bb[VP] + max(0, i - 1))
                        for k in range(srcchunks):
                            ins = t.matmul(
                                ps[:], wbf[:, k * wcols * 128 + c * 128:
                                           k * wcols * 128 + (c + 1) * 128],
                                src[:, k * S + n * 512:k * S + (n + 1) * 512],
                                start=(k == 0), stop=(k == srcchunks - 1))
                        TP.inc(ins, 1)
                        i += 1
            block.tensor(_)

            def _(v):
                i = 0
                for c in range(nout):
                    for n in range(NS):
                        ps = psA if i % 2 == 0 else psB
                        v.wait_ge(TP.h, bb[TP] + i + 1)
                        dst_write(v, c, n, ps)
                        i += 1
            block.vector(_)

        def _(v):
            v.wait_ge(LS.h, ls_xspill)
        block.vector(_)
        emit_proj(psA, psB, WkvBF, ND, h1T, ND, ND,
                  lambda v, c, n, ps: VP.inc(v.tensor_copy(
                      kvT[:, c * S + n * 512:c * S + (n + 1) * 512], ps[:]), 1))
        dbg_dump("d_kv", kvT[:], [(VP, VP.n)])
        emit_proj(psA, psB, WdkvBF, NDL, kvT, ND, NDL,
                  lambda v, c, n, ps: VP.inc(v.tensor_copy(
                      latentT[:, c * S + n * 512:c * S + (n + 1) * 512], ps[:]), 1))
        dbg_dump("d_lat", latentT[:], [(VP, VP.n)])
        barrier()
        g_c0p.close()
        g_kv.close()
        ck(6)
        g_wkv.close()

        g_attn = mkstack()
        attnoutT = SB(g_attn, "attnoutT", [128, ND * S], BF16)
        g_qkr = mkstack()
        qA = SB(g_qkr, "qA", [128, 4 * S], BF16, side="right")
        qB = SB(g_qkr, "qB", [128, 4 * S], BF16, side="right")
        krA = SB(g_qkr, "krA", [128, 4 * S], BF16, side="right")
        krB = SB(g_qkr, "krB", [128, 4 * S], BF16, side="right")
        vton = SB(g_qkr, "vton", [128, 8 * NT * 65], BF16, side="right")
        expT = [SB(g_qkr, f"expT{i}", [128, 1024], BF16, side="right")
                for i in range(2)]

        qmx = SB(g_qkr, "qmx", [128, 4], F32, side="right")
        kmx = SB(g_qkr, "kmx", [128, 4], F32, side="right")
        rnorm = SB(g_qkr, "rnorm", [128, 1024], F32, side="right")
        rsrow = rnorm[64:65, :]
        rbt = rnorm[0:64, :]

        def emit_half_qkr(hf, psA, psB, ps_qb):
            sub = 6 if hf == 0 else 8
            bq = snap()

            def _(t):
                i = 0
                for cl in range(4):
                    cg = 4 * hf + cl
                    for n in range(NS):
                        ps = psA if i % 2 == 0 else psB
                        t.wait_ge(VP.h, bq[VP] + max(0, i - 1))
                        for k in range(ND):
                            ins = t.matmul(
                                ps[:], WqBF[:, k * D + cg * 128:k * D + (cg + 1) * 128],
                                h1T[:, k * S + n * 512:k * S + (n + 1) * 512],
                                start=(k == 0), stop=(k == ND - 1))
                        TP.inc(ins, 1)
                        i += 1
            block.tensor(_)

            def _(v):
                i = 0
                for cl in range(4):
                    for n in range(NS):
                        ps = psA if i % 2 == 0 else psB
                        v.wait_ge(TP.h, bq[TP] + i + 1)
                        sl = slice(cl * S + n * 512, cl * S + (n + 1) * 512)
                        dsl = slice(n * 512, (n + 1) * 512)
                        v.tensor_tensor(qA[:, sl], ps[:], DRA[:, dsl], ALU.mult)
                        VP.inc(v.tensor_tensor(qB[:, sl], ps[:], DRB[:, dsl],
                                               ALU.mult), 1)
                        i += 1
            block.vector(_)
            if debug and hf == dump_half:
                dbg_dump("d_qa", qA[:], [(VP, VP.n)])
            ck(sub + 0.1)

            bk = snap()

            def _(t):
                i = 0
                for cl in range(4):
                    cg = 4 * hf + cl
                    for n in range(NS):
                        ps = psA if i % 2 == 0 else psB
                        t.wait_ge(VP.h, bk[VP] + max(0, i - 1))
                        for k in range(NDL):
                            ins = t.matmul(
                                ps[:], WupkBF[:, k * D + cg * 128:k * D + (cg + 1) * 128],
                                latentT[:, k * S + n * 512:k * S + (n + 1) * 512],
                                start=(k == 0), stop=(k == NDL - 1))
                        TP.inc(ins, 1)
                        i += 1
            block.tensor(_)

            def _(v):
                i = 0
                for cl in range(4):
                    for n in range(NS):
                        ps = psA if i % 2 == 0 else psB
                        v.wait_ge(TP.h, bk[TP] + i + 1)
                        sl = slice(cl * S + n * 512, cl * S + (n + 1) * 512)
                        dsl = slice(n * 512, (n + 1) * 512)
                        v.tensor_tensor(krA[:, sl], ps[:], DRB[:, dsl], ALU.mult)
                        VP.inc(v.tensor_tensor(krB[:, sl], ps[:], DRA[:, dsl],
                                               ALU.mult), 1)
                        i += 1
            block.vector(_)
            if debug and hf == dump_half:
                dbg_dump("d_kra", qA[:], [(VP, VP.n)])
            ck(sub + 0.2)

            bv = snap()

            def _(t):
                i = 0
                for hl in range(8):
                    hgl = 8 * hf + hl
                    for gq in range(2):
                        ps = psA if i % 2 == 0 else psB
                        t.wait_ge(VP.h, bv[VP] + (i if i >= 2 else 0))
                        for tc8 in range(8):
                            tc = gq * 8 + tc8
                            for k in range(NDL):
                                ins = t.matmul(
                                    ps[:, tc8 * 64:(tc8 + 1) * 64],
                                    latentT[:, k * S + tc * 128:k * S + (tc + 1) * 128],
                                    WupvBF[:, k * D + hgl * 64:k * D + hgl * 64 + 64],
                                    start=(tc8 == 0 and k == 0), stop=(k == NDL - 1))
                        TP.inc(ins, 1)
                        i += 1
            block.tensor(_)

            def _(v):
                VP.inc(v.memset(vton[:], 1.0), 1)
                i = 0
                for hl in range(8):
                    for gq in range(2):
                        ps = psA if i % 2 == 0 else psB
                        v.wait_ge(TP.h, bv[TP] + i + 1)
                        dst = vton[:, hl * NT * 65 + gq * 8 * 65:
                                   hl * NT * 65 + (gq + 1) * 8 * 65]
                        VP.inc(v.tensor_copy(
                            dst.rearrange("p (t c) -> p t c", c=65)[:, :, 0:64],
                            ps[:].rearrange("p (t c) -> p t c", c=64)), 1)
                        i += 1
            block.vector(_)
            if debug and hf == dump_half:
                def _(g):
                    g.wait_ge(VP.h, VP.n)
                    DS.inc(g.dma_start(out=dbg["d_xq"][:, 0:4 * S], in_=qA[:]), 16)
                block.gpsimd(_)
            ck(sub + 0.3)

            def _(v):
                for cl in range(4):
                    for (srcA, srcB, acc) in ((qA, qB, qmx), (krA, krB, kmx)):
                        for half2 in range(2):
                            sl = slice(cl * S + half2 * 1024,
                                       cl * S + (half2 + 1) * 1024)
                            v.tensor_tensor(expT[0][:], srcA[:, sl], srcB[:, sl],
                                            ALU.mult)
                            v.tensor_reduce(
                                sclB[:, SC_T0 + half2:SC_T0 + half2 + 1],
                                expT[0][:], AX.X, ALU.max)
                        v.drain()
                        v.tensor_tensor(acc[:, cl:cl + 1],
                                        sclB[:, SC_T0:SC_T0 + 1],
                                        sclB[:, SC_T1:SC_T1 + 1], ALU.max)
                VP.inc(v.drain(), 1)
            block.vector(_)
            ck(sub + 0.4)
            vp_bnd = VP.n

            def _(t):
                t.wait_ge(VP.h, vp_bnd)
                for hl in range(8):
                    h = 8 * hf + hl
                    r0 = (hl % 2) * 64
                    cl = hl // 2
                    t.matmul(ps_qb[0:1, h:h + 1], qmx[r0:r0 + 64, cl:cl + 1],
                             onesc[r0:r0 + 64, :], start=True, stop=True)
                    ins = t.matmul(ps_qb[0:1, 16 + h:17 + h], kmx[r0:r0 + 64, cl:cl + 1],
                                   onesc[r0:r0 + 64, :], start=True, stop=True)
                TP.inc(ins, 1)
            block.tensor(_)
            tp_qb = TP.n
            hs = slice(8 * hf, 8 * hf + 8)

            def _(v):
                v.wait_ge(TP.h, tp_qb)
                v.tensor_copy(qkrow[0:1, 0:32], ps_qb[0:1, 0:32])
                VP.inc(v.drain(), 1)
                v.tensor_tensor(negMrow[0:1, hs],
                                qkrow[0:1, 8 * hf:8 * hf + 8],
                                qkrow[0:1, 16 + 8 * hf:16 + 8 * hf + 8],
                                ALU.mult)
                VP.inc(v.drain(), 1)
                v.tensor_scalar_max(negMrow[0:1, hs], negMrow[0:1, hs], 1e-18)
                VP.inc(v.drain(), 1)
            block.vector(_)
            vp_nm = VP.n

            def _(s):
                s.wait_ge(VP.h, vp_nm)
                SPm.inc(s.activation(qkrow[0:1, hs], negMrow[0:1, hs], AF.Ln), 1)
                s.wait_ge(SPm.h, SPm.n)
                SPm.inc(s.activation(negMrow[0:1, hs], qkrow[0:1, hs],
                                     AF.Exp, scale=0.5), 1)
            block.scalar(_)
            sp_nm = SPm.n

            def _(v):
                v.wait_ge(SPm.h, sp_nm)
                v.tensor_scalar_mul(negMrow[0:1, hs], negMrow[0:1, hs], -1.05)
                VP.inc(v.drain(), 1)
            block.vector(_)
            vp_nm2 = VP.n

            def _(t):
                t.wait_ge(VP.h, vp_nm2)
                TP.inc(t.matmul(ps_qb[:, 8 * hf:8 * hf + 8], onesr[0:1, :],
                                negMrow[0:1, hs], start=True, stop=True), 1)
            block.tensor(_)
            tp_nm = TP.n

            def _(v):
                v.wait_ge(TP.h, tp_nm)
                VP.inc(v.tensor_copy(negMall[:, hs], ps_qb[:, 8 * hf:8 * hf + 8]), 1)
            block.vector(_)

        def segments(qb, ks):
            w0 = qb * 1024
            kpos = ks * 128
            segs = []
            bend = min(kpos, w0 + 1024)
            if bend > w0:
                segs.append((w0, bend, "B"))
            if w0 <= kpos < w0 + 1024:
                segs.append((kpos, kpos + 128, "D"))
            astart = max(kpos + 128, w0)
            if astart < w0 + 1024:
                segs.append((astart, w0 + 1024, "A"))
            out = []
            for (a0, a1, ver) in segs:
                p = a0
                while p < a1:
                    lim = w0 + 512 if p < w0 + 512 else w0 + 1024
                    q = min(a1, lim, p + 512)
                    out.append((p, q, ver))
                    p = q
            return out

        def emit_half_softmax(hf, sc0, sc1, pso):
            for hl in range(8):
                h = 8 * hf + hl
                r0 = (hl % 2) * 64
                cl = hl // 2
                coff = cl * S
                aoff = (4 * hf + cl) * S
                for qb in range(2):
                    it = (hf * 8 + hl) * 2 + qb
                    w0 = qb * 1024
                    exb = it * 16
                    scb = it * 16
                    avb = it * 16

                    def _(t, hl=hl, r0=r0, coff=coff, qb=qb, it=it,
                          w0=w0, exb=exb):
                        t.wait_ge(sNM3.h, it)
                        for ks in range(16):
                            bank = sc0 if ks % 2 == 0 else sc1
                            if ks >= 2:
                                t.wait_ge(sEX.h, exb + ks - 1)
                            last = None
                            for (a0, a1, ver) in segments(qb, ks):
                                qop = qA if ver in ("A", "D") else qB
                                krop = krA if ver in ("A", "D") else krB
                                last = t.matmul(
                                    bank[:, a0 - w0:a1 - w0],
                                    krop[r0:r0 + 64, coff + ks * 128:coff + (ks + 1) * 128],
                                    qop[r0:r0 + 64, coff + a0:coff + a1],
                                    start=True, stop=True)
                            sSC.inc(last, 1)
                            if ks >= 1:
                                t.wait_ge(sEX.h, exb + ks)
                                if ks == 1:
                                    t.wait_ge(sNM4.h, it)
                                for n in range(2):
                                    ins = t.matmul(
                                        pso[0:65, n * 512:(n + 1) * 512],
                                        vton[:, hl * NT * 65 + (ks - 1) * 65:
                                             hl * NT * 65 + (ks - 1) * 65 + 65],
                                        expT[(ks - 1) % 2][:, n * 512:(n + 1) * 512],
                                        start=(ks - 1 == 0), stop=False)
                                sAV.inc(ins, 1)
                        t.wait_ge(sEX.h, exb + 16)
                        for n in range(2):
                            ins = t.matmul(
                                pso[0:65, n * 512:(n + 1) * 512],
                                vton[:, hl * NT * 65 + 15 * 65:hl * NT * 65 + 15 * 65 + 65],
                                expT[1][:, n * 512:(n + 1) * 512],
                                start=False, stop=True)
                        sAV.inc(ins, 1)
                        t.wait_ge(sNM1.h, it + 1)
                        for n in range(2):
                            ins = t.matmul(sc0[0:64, n * 512:(n + 1) * 512],
                                           onesr[64:65, 0:64],
                                           rnorm[64:65, n * 512:(n + 1) * 512],
                                           start=True, stop=True)
                        sNM2.inc(ins, 1)
                    block.tensor(_)

                    def _(v, r0=r0, aoff=aoff, qb=qb, it=it, w0=w0,
                          scb=scb, avb=avb):
                        for ks in range(16):
                            bank = sc0 if ks % 2 == 0 else sc1
                            v.wait_ge(sSC.h, scb + ks + 1)
                            kpos = ks * 128
                            if w0 <= kpos < w0 + 1024:
                                ins = v.tensor_tensor(
                                    bank[:, kpos - w0:kpos - w0 + 128],
                                    bank[:, kpos - w0:kpos - w0 + 128],
                                    cmaskt[:], ALU.mult)
                                sDF.inc(ins, 1)
                            else:
                                sDF.inc(v.nop(), 1)
                        v.wait_ge(sAV.h, avb + 16)
                        v.reciprocal(rsrow, pso[64:65, 0:1024])
                        sNM1.inc(v.drain(), 1)
                        v.wait_ge(sNM3.h, it + 1)
                        ins = v.tensor_tensor(
                            attnoutT[r0:r0 + 64, aoff + w0:aoff + w0 + 1024],
                            pso[0:64, 0:1024], rbt, ALU.mult)
                        sNM4.inc(ins, 1)
                    block.vector(_)

                    def _(s, h=h, it=it, exb=exb, avb=avb, scb=scb):
                        for ks in range(16):
                            bank = sc0 if ks % 2 == 0 else sc1
                            s.wait_ge(sDF.h, scb + ks + 1)
                            if ks >= 2:
                                s.wait_ge(sAV.h, avb + ks - 1)
                            sEX.inc(s.activation(expT[ks % 2][:], bank[:, 0:1024],
                                                 AF.Exp, bias=negMall[:, h:h + 1]), 1)
                        s.wait_ge(sNM2.h, it + 1)
                        sNM3.inc(s.copy(rbt, sc0[0:64, 0:1024]), 1)
                    block.scalar(_)
                    if debug and hf == 0 and hl == 0 and qb == 0:
                        dbg_dump("d_rnorm", rnorm[:], [(sNM4, 1)])
                        dbg_dump("d_expt", expT[1][:], [(sNM4, 1)])
                        dbg_dump("d_vton", vton[:, 0:NT * 65], [(sNM4, 1)])
                        ck(7.5)

        for hf in range(2):
            g_c1 = mkstack()
            psA1 = PSU(g_c1, f"psA1_{hf}", [128, 512])
            psB1 = PSU(g_c1, f"psB1_{hf}", [128, 512])
            ps_qb = PSU(g_c1, f"ps_qb_{hf}", [128, 512])
            emit_half_qkr(hf, psA1, psB1, ps_qb)
            barrier()
            g_c1.close()
            if hf == dump_half and debug:
                dbg_dump("d_qmx", qmx[:], [(BR, BR.n)])
                dbg_dump("d_kmx", kmx[:], [(BR, BR.n)])
                dbg_dump("d_qbrow", qkrow[:].rearrange("p (a b) -> p a b", a=2)[:, 0, :], [(BR, BR.n)])
                dbg_dump("d_negall", negMall[:], [(BR, BR.n)])
            ck(7 if hf == 0 else 9)
            g_c2 = mkstack()
            sc0 = PSU(g_c2, f"sc0_{hf}", [128, 1024])
            sc1 = PSU(g_c2, f"sc1_{hf}", [128, 1024])
            pso = PSU(g_c2, f"pso_{hf}", [128, 1024])
            emit_half_softmax(hf, sc0, sc1, pso)
            barrier()
            g_c2.close()
            if hf == dump_half and debug:
                dbg_dump("d_qa", qA[:], [(BR, BR.n)])
                dbg_dump("d_kra", krA[:], [(BR, BR.n)])
                dbg_dump("d_negm", negMrow[:], [(BR, BR.n)])
                dbg_dump("d_attn", attnoutT[:], [(BR, BR.n)])
            ck(8 if hf == 0 else 10)

        dbg_dump("d_attn", attnoutT[:], [(BR, BR.n)])
        g_qkr.close()
        g_lat.close()
        g_wb.close()
        g_dr.close()
        g_h1.close()

        g_x2 = mkstack()
        xT2 = SB(g_x2, "xT2", [128, ND * S], F32, side="right")
        g_d0 = mkstack()
        WoBF = SB(g_d0, "WoBF", [128, ND * D], BF16)
        wodst = SB(g_d0, "wost", [128, D], F32)
        psA2 = PSU(g_d0, "psA2", [128, 512])
        psB2 = PSU(g_d0, "psB2", [128, 512])

        b = snap()

        def _(sync):
            sync.wait_ge(LS.h, ls_xspill)
            for c in range(ND):
                LD.inc(sync.dma_start(out=xT2[:, c * S:(c + 1) * S], in_=sp_x[c]), 16)
            for k in range(ND):
                if k >= 1:
                    sync.wait_ge(VP.h, b[VP] + k)
                LD.inc(sync.dma_start(out=wodst[:], in_=Wo_in[k * 128:(k + 1) * 128, :]), 16)
        block.sync(_)

        def _(v):
            for k in range(ND):
                v.wait_ge(LD.h, b[LD] + ND * 16 + (k + 1) * 16)
                VP.inc(v.tensor_copy(WoBF[:, k * D:(k + 1) * D], wodst[:]), 1)
        block.vector(_)
        vp_wo = VP.n

        b2 = snap()

        def _(t):
            t.wait_ge(VP.h, vp_wo)
            i = 0
            for c in range(ND):
                for n in range(NS):
                    ps = psA2 if i % 2 == 0 else psB2
                    if i >= 2:
                        t.wait_ge(VP.h, b2[VP] + i - 1)
                    for k in range(ND):
                        ins = t.matmul(
                            ps[:], WoBF[:, k * D + c * 128:k * D + (c + 1) * 128],
                            attnoutT[:, k * S + n * 512:k * S + (n + 1) * 512],
                            start=(k == 0), stop=(k == ND - 1))
                    TP.inc(ins, 1)
                    i += 1
        block.tensor(_)

        def _(v):
            i = 0
            for c in range(ND):
                v.wait_ge(LD.h, b[LD] + (c + 1) * 16)
                for n in range(NS):
                    ps = psA2 if i % 2 == 0 else psB2
                    v.wait_ge(TP.h, b2[TP] + i + 1)
                    sl = slice(c * S + n * 512, c * S + (n + 1) * 512)
                    VP.inc(v.scalar_tensor_tensor(
                        xT2[:, sl], ps[:], gaP[:, c:c + 1], xT2[:, sl],
                        ALU.mult, ALU.add), 1)
                    i += 1
        block.vector(_)
        barrier()
        g_d0.close()
        g_attn.close()
        ck(11)
        dbg_dump("d_xmid", xT2[:], [(BR, BR.n)])

        g_xq = mkstack()
        xqT = SB(g_xq, "xqT", [128, ND * S], BF16)
        g_h2 = mkstack()
        h2b = SB(g_h2, "h2b", [128, ND * S], F32)
        hxm = SB(g_h2, "hxm", [128, 8], F32)
        emit_layernorm("L2", xT2, cbP, modT[:, 24:32], h2b, absmax_cols=hxm)
        vp_e0 = VP.n

        def _(s):
            s.wait_ge(VP.h, vp_e0)
            for c in range(ND):
                LS.inc(s.dma_start(out=sp_x[c], in_=xT2[:, c * S:(c + 1) * S]), 16)
        block.scalar(_)
        ls_x2 = LS.n
        barrier()

        g_e0p = mkstack()
        ps_e = PSU(g_e0p, "ps_e", [128, 512])

        def colmax2(srccols, dstcol):
            def _(v):
                v.tensor_reduce(sclB[:, SC_T0:SC_T0 + 1], srccols, AX.X, ALU.max)
                VP.inc(v.drain(), 1)
            block.vector(_)
            vpr = VP.n

            def _(t):
                t.wait_ge(VP.h, vpr)
                TP.inc(t.matmul(ps_e[0:1, 0:128], sclB[:, SC_T0:SC_T0 + 1],
                                identt[:], start=True, stop=True), 1)
            block.tensor(_)
            tpr = TP.n

            def _(v):
                v.wait_ge(TP.h, tpr)
                v.tensor_reduce(scal[0:1, dstcol:dstcol + 1],
                                ps_e[0:1, 0:128], AX.X, ALU.max)
                VP.inc(v.drain(), 1)
            block.vector(_)

        def bcast2(srccol, dstcol):
            bb = snap()

            def _(t):
                t.wait_ge(VP.h, bb[VP])
                TP.inc(t.matmul(ps_e[:, 128 + dstcol:129 + dstcol], onesr[0:1, :],
                                scal[0:1, srccol:srccol + 1], start=True, stop=True), 1)
            block.tensor(_)
            tpr = TP.n

            def _(v):
                v.wait_ge(TP.h, tpr)
                VP.inc(v.tensor_copy(sclB[:, dstcol:dstcol + 1],
                                     ps_e[:, 128 + dstcol:129 + dstcol]), 1)
            block.vector(_)

        def allreduce(idx, pack_fn, unpack_fn):
            def _(v):
                pack_fn(v)
            block.vector(_)
            vpk = VP.n

            def _(g):
                g.wait_ge(VP.h, vpk)
                LG.inc(g.dma_start(out=cc_i[idx], in_=ccs[0:1, 0:2]), 16)
                g.wait_ge(LG.h, LG.n)
                CC.inc(g.collective_compute(
                    "AllReduce", ALU.max, replica_groups=[list(range(8))],
                    ins=[cc_i[idx].opt()], outs=[cc_o[idx].opt()]), 1)
                g.wait_ge(CC.h, CC.n)
                LG.inc(g.dma_start(out=ccs[0:1, 2:4], in_=cc_o[idx]), 16)
            block.gpsimd(_)
            lgr = LG.n

            def _(v):
                v.wait_ge(LG.h, lgr)
                unpack_fn(v)
            block.vector(_)

        colmax2(hxm[:], 0)

        def pack1(v):
            v.tensor_copy(ccs[0:1, 0:1], scal[0:1, 0:1])
            v.tensor_copy(ccs[0:1, 1:2], scal[0:1, 0:1])
            VP.inc(v.drain(), 1)

        def unpack1(v):
            v.tensor_scalar(scal[0:1, 1:2], ccs[0:1, 2:3], 1.0 / QMAX, 1e-12,
                            ALU.mult, ALU.max)
            VP.inc(v.drain(), 1)
            v.reciprocal(scal[0:1, 2:3], scal[0:1, 1:2])
            VP.inc(v.drain(), 1)
        allreduce(0, pack1, unpack1)
        bcast2(1, SC_SX)
        bcast2(2, SC_RSX)
        vp_sx = VP.n
        dbg_dump("d_scal", scal[:], [(VP, vp_sx)])

        def _(v):
            v.wait_ge(VP.h, vp_sx)
            for c in range(ND):
                hsl = h2b[:, c * S:(c + 1) * S]
                hi32 = hsl.bitcast(I32)
                v.tensor_scalar(hi32, hsl, sclB[:, SC_RSX:SC_RSX + 1], None,
                                ALU.mult)
                VP.inc(v.drain(), 1)
                VP.inc(v.tensor_copy(xqT[:, c * S:(c + 1) * S], hi32), 1)
            VP.inc(v.drain(), 1)
        block.vector(_)
        vp_xq = VP.n
        dbg_dump("d_xq", xqT[:], [(VP, vp_xq)])
        barrier()
        ck(12)
        g_h2.close()
        g_x2.close()

        g_wu = mkstack()
        wupT = SB(g_wu, "wupT", [128, ND * DI], BF16, side="right")
        g_wg = mkstack()
        wgT = SB(g_wg, "wgT", [128, ND * DI], BF16, side="right")
        g_f1 = mkstack()
        spb = [SB(g_f1, f"spb{i}", [128, S], F16, side="right") for i in range(2)]
        mxu = SB(g_f1, "mxu", [128, NI], F32, side="right")
        mxg = SB(g_f1, "mxg", [128, NI], F32, side="right")
        psA3 = PSU(g_f1, "psA3", [128, 512])
        psB3 = PSU(g_f1, "psB3", [128, 512])

        b = snap()

        def _(sync):
            sync.wait_ge(LG.h, lg_f0)
            sync.wait_ge(LS.h, ls_x2)
            for ds in range(ND):
                for p in range(3):
                    LD.inc(sync.dma_start_transpose(
                        out=wupT[:, ds * DI + p * 1024:ds * DI + (p + 1) * 1024],
                        in_=sp_wu[p * 1024:(p + 1) * 1024, ds * 128:(ds + 1) * 128]), 16)
            for ds in range(ND):
                for p in range(3):
                    LD.inc(sync.dma_start_transpose(
                        out=wgT[:, ds * DI + p * 1024:ds * DI + (p + 1) * 1024],
                        in_=sp_wg[p * 1024:(p + 1) * 1024, ds * 128:(ds + 1) * 128]), 16)
        block.sync(_)
        ld_wut = b[LD] + 24 * 16
        ld_wgt = b[LD] + 48 * 16
        dbg_dump("d_wut", wupT[:, 0:DI], [(LD, ld_wut)])
        if debug:
            def _(g):
                g.wait_ge(LG.h, lg_f0)
                DS.inc(g.dma_start(out=dbg["d_spwu"], in_=sp_wu[0:128, :]), 16)
            block.gpsimd(_)

        def emit_f1(wT, ld_w, sp_dst, mxdst):
            bb = snap()

            def _(t):
                t.wait_ge(LD.h, ld_w)
                t.wait_ge(VP.h, vp_xq)
                i = 0
                for ic in range(NI):
                    for n in range(NS):
                        ps = psA3 if i % 2 == 0 else psB3
                        t.wait_ge(VP.h, bb[VP] + max(0, 2 * (i - 1)))
                        for k in range(ND):
                            ins = t.matmul(
                                ps[:], wT[:, k * DI + ic * 128:k * DI + (ic + 1) * 128],
                                xqT[:, k * S + n * 512:k * S + (n + 1) * 512],
                                start=(k == 0), stop=(k == ND - 1))
                        TP.inc(ins, 1)
                        i += 1
            block.tensor(_)

            def _(v):
                i = 0
                for ic in range(NI):
                    for n in range(NS):
                        ps = psA3 if i % 2 == 0 else psB3
                        v.wait_ge(TP.h, bb[TP] + i + 1)
                        if ic >= 2 and n == 0:
                            v.wait_ge(LS.h, bb[LS] + (ic - 1) * 16)
                        sl = slice(n * 512, (n + 1) * 512)
                        v.tensor_scalar(spb[ic % 2][:, sl], ps[:],
                                        sclB[:, SC_SX:SC_SX + 1], None,
                                        ALU.mult)
                        VP.inc(v.tensor_reduce(
                            mxdst[:, ic:ic + 1] if n == 0 else sclB[:, SC_T1:SC_T1 + 1],
                            ps[:], AX.X, ALU.max,
                            apply_absolute_value=True), 1)
                        if n > 0:
                            v.drain()
                            v.tensor_tensor(mxdst[:, ic:ic + 1], mxdst[:, ic:ic + 1],
                                            sclB[:, SC_T1:SC_T1 + 1], ALU.max)
                        if n == NS - 1:
                            VP.inc(v.drain(), 1)
                        else:
                            VP.inc(v.nop(), 1)
                        i += 1
            block.vector(_)
            vp0 = bb[VP]

            def _(s):
                for ic in range(NI):
                    s.wait_ge(VP.h, vp0 + 2 * NS * (ic + 1))
                    LS.inc(s.dma_start(out=sp_dst[ic], in_=spb[ic % 2][:]), 16)
            block.scalar(_)

        ls_upsp0 = LS.n
        emit_f1(wupT, ld_wut, sp_up, mxu)
        ls_gsp0 = LS.n
        emit_f1(wgT, ld_wgt, sp_gate, mxg)
        barrier()

        colmax2(mxu[:], 3)
        colmax2(mxg[:], 4)

        def pack2(v):
            v.tensor_tensor(ccs[0:1, 0:1], scal[0:1, 3:4], scal[0:1, 1:2],
                            ALU.mult)
            v.tensor_tensor(ccs[0:1, 1:2], scal[0:1, 4:5], scal[0:1, 1:2],
                            ALU.mult)
            VP.inc(v.drain(), 1)

        def unpack2(v):
            v.tensor_scalar(scal[0:1, 3:4], ccs[0:1, 2:3], 1.0 / QMAX, 1e-12,
                            ALU.mult, ALU.max)
            v.tensor_scalar(scal[0:1, 4:5], ccs[0:1, 3:4], 1.0 / QMAX, 1e-12,
                            ALU.mult, ALU.max)
            VP.inc(v.drain(), 1)
            v.reciprocal(scal[0:1, 5:6], scal[0:1, 3:4])
            v.reciprocal(scal[0:1, 6:7], scal[0:1, 4:5])
            v.tensor_scalar_mul(scal[0:1, 7:8], scal[0:1, 4:5], 1.0 / 65536.0)
            VP.inc(v.drain(), 3)
        allreduce(1, pack2, unpack2)
        bcast2(5, SC_RSUP)
        bcast2(4, SC_SG)
        bcast2(6, SC_RSG)
        bcast2(7, SC_KAP0)
        barrier()
        g_f1.close()
        ck(13)
        g_wg.close()
        g_wu.close()
        g_xq.close()
        g_f0.close()

        g_mid = mkstack()
        mid = SB(g_mid, "mid", [128, NI * S], BF16)
        g_wd = mkstack()
        wdT = SB(g_wd, "wdT", [128, NI * D], BF16)
        g_f3 = mkstack()
        gbuf = [SB(g_f3, f"gbuf{i}", [128, S], F16, side="right") for i in range(2)]
        gi32 = SB(g_f3, "gi32", [128, S], I32, side="right")
        ri32 = SB(g_f3, "ri32", [128, S], I32, side="right")
        rmf = SB(g_f3, "rmf", [128, S], F32, side="right")
        sigb = [SB(g_f3, f"sigb{i}", [128, S], F32, side="right") for i in range(2)]
        mxg2 = SB(g_f3, "mxg2", [128, NI], F32, side="right")

        b = snap()

        def _(sync):
            for i in range(NI):
                LD.inc(sync.dma_start_transpose(
                    out=wdT[:, i * D:(i + 1) * D],
                    in_=sp_wd[:, i * 128:(i + 1) * 128]), 16)
            for i in range(NI):
                if i >= 2:
                    sync.wait_ge(VP.h, b[VP] + 4 * (i - 1))
                sync.wait_ge(LS.h, ls_gsp0 + (i + 1) * 16)
                LD.inc(sync.dma_start(out=gbuf[i % 2][:], in_=sp_gate[i]), 16)
        block.sync(_)
        ld_wdt = b[LD] + NI * 16

        def _(v):
            for i in range(NI):
                v.wait_ge(LD.h, ld_wdt + (i + 1) * 16)
                if i >= 1:
                    v.wait_ge(SPm.h, b[SPm] + i)
                v.tensor_scalar(gi32[:], gbuf[i % 2][:],
                                sclB[:, SC_RSG:SC_RSG + 1], None, ALU.mult)
                VP.inc(v.drain(), 1)
                v.wait_ge(SPm.h, b[SPm] + i + 1)
                v.tensor_scalar(ri32[:], sigb[i % 2][:], 65536.0, None, ALU.mult)
                v.drain()
                v.tensor_scalar_min(rmf[:], ri32[:], 65535.0)
                v.drain()
                msl = mid[:, i * S:(i + 1) * S].bitcast(F16)
                v.scalar_tensor_tensor(msl, gi32[:],
                                       sclB[:, SC_KAP0:SC_KAP0 + 1], rmf[:],
                                       ALU.mult, ALU.mult)
                VP.inc(v.drain(), 1)
                VP.inc(v.tensor_reduce(mxg2[:, i:i + 1], msl, AX.X, ALU.max,
                                       apply_absolute_value=True), 1)
                VP.inc(v.nop(), 1)
        block.vector(_)

        def _(s):
            for i in range(NI):
                s.wait_ge(VP.h, b[VP] + 4 * i + 1)
                SPm.inc(s.activation(sigb[i % 2][:], gi32[:], AF.Sigmoid,
                                     scale=sclB[:, SC_SG:SC_SG + 1]), 1)
        block.scalar(_)
        barrier()

        colmax2(mxg2[:], 8)

        def pack3(v):
            v.tensor_copy(ccs[0:1, 0:1], scal[0:1, 8:9])
            v.tensor_copy(ccs[0:1, 1:2], scal[0:1, 8:9])
            VP.inc(v.drain(), 1)

        def unpack3(v):
            v.tensor_scalar(scal[0:1, 9:10], ccs[0:1, 2:3], 1.0 / QMAX, 1e-12,
                            ALU.mult, ALU.max)
            VP.inc(v.drain(), 1)
            v.reciprocal(scal[0:1, 10:11], scal[0:1, 9:10])
            v.tensor_tensor(scal[0:1, 11:12], scal[0:1, 9:10],
                            scal[0:1, 3:4], ALU.mult)
            VP.inc(v.drain(), 2)
        allreduce(2, pack3, unpack3)
        bcast2(10, SC_RSG2)
        bcast2(11, SC_KAP3)
        barrier()
        ck(14)

        b = snap()

        def _(sync):
            for i in range(NI):
                if i >= 2:
                    sync.wait_ge(VP.h, b[VP] + 4 * (i - 1))
                sync.wait_ge(LS.h, ls_upsp0 + (i + 1) * 16)
                LD.inc(sync.dma_start(out=gbuf[i % 2][:], in_=sp_up[i]), 16)
        block.sync(_)

        def _(v):
            for i in range(NI):
                v.wait_ge(LD.h, b[LD] + (i + 1) * 16)
                msl16 = mid[:, i * S:(i + 1) * S].bitcast(F16)
                v.tensor_scalar(gi32[:], msl16, sclB[:, SC_RSG2:SC_RSG2 + 1],
                                None, ALU.mult)
                v.tensor_scalar(ri32[:], gbuf[i % 2][:],
                                sclB[:, SC_RSUP:SC_RSUP + 1], None, ALU.mult)
                VP.inc(v.drain(), 1)
                v.scalar_tensor_tensor(msl16, gi32[:],
                                       sclB[:, SC_KAP3:SC_KAP3 + 1], ri32[:],
                                       ALU.mult, ALU.mult)
                VP.inc(v.drain(), 1)
                VP.inc(v.tensor_reduce(mxg2[:, i:i + 1], msl16, AX.X, ALU.max,
                                       apply_absolute_value=True), 1)
                VP.inc(v.nop(), 1)
        block.vector(_)
        barrier()

        colmax2(mxg2[:], 12)

        def pack4(v):
            v.tensor_copy(ccs[0:1, 0:1], scal[0:1, 12:13])
            v.tensor_copy(ccs[0:1, 1:2], scal[0:1, 12:13])
            VP.inc(v.drain(), 1)

        def unpack4(v):
            v.tensor_scalar(scal[0:1, 13:14], ccs[0:1, 2:3], 1.0 / QMAX, 1e-12,
                            ALU.mult, ALU.max)
            VP.inc(v.drain(), 1)
            v.reciprocal(scal[0:1, 14:15], scal[0:1, 13:14])
            VP.inc(v.drain(), 1)
        allreduce(3, pack4, unpack4)
        bcast2(14, SC_RSO)
        bcast2(13, SC_SO)

        b = snap()

        def _(v):
            v.wait_ge(VP.h, b[VP])
            VP.inc(v.tensor_scalar(gbsoP[:], gbP[:], sclB[:, SC_SO:SC_SO + 1],
                                   None, ALU.mult), 1)
            for i in range(NI):
                msl16 = mid[:, i * S:(i + 1) * S].bitcast(F16)
                v.tensor_scalar(gi32[:], msl16, sclB[:, SC_RSO:SC_RSO + 1],
                                None, ALU.mult)
                VP.inc(v.drain(), 1)
                VP.inc(v.tensor_copy(mid[:, i * S:(i + 1) * S], gi32[:]), 1)
        block.vector(_)
        barrier()
        g_f3.close()
        ck(15)
        dbg_dump("d_mid0", mid[:, 0:S], [(BR, BR.n)])
        dbg_dump("d_scal2", scal[:], [(BR, BR.n)])

        g_f6 = mkstack()
        xv = [SB(g_f6, f"xv{i}", [128, S], F32, side="right") for i in range(2)]
        otc = [SB(g_f6, f"otc{i}", [128, S], F32, side="right") for i in range(2)]
        otok = [SB(g_f6, f"otok{i}", [128, S], F32, side="right") for i in range(2)]
        psA4 = PSU(g_f6, "psA4", [128, 512])
        psB4 = PSU(g_f6, "psB4", [128, 512])
        psT4 = PSU(g_f6, "psT4", [128, 512])
        psT5 = PSU(g_f6, "psT5", [128, 512])

        b = snap()
        ydst = y.rearrange("(n p) d -> p n d", p=128)
        st_ld = {}
        ld_xv = {}

        def _(sync):
            sync.wait_ge(LS.h, ls_x2)

            def emit_stores(m):
                for g in range(4):
                    sync.wait_ge(VP.h, b[VP] + m * 8 + NS + g + 1)
                    for j in range(4):
                        tt = g * 4 + j
                        LD.inc(sync.dma_start(
                            out=y[tt * 128:(tt + 1) * 128, m * 128:(m + 1) * 128],
                            in_=otok[m % 2][:, tt * 128:(tt + 1) * 128]), 16)
                    st_ld[(m, g)] = LD.n

            for m in range(ND):
                if m >= 2:
                    emit_stores(m - 2)
                    sync.wait_ge(VP.h, b[VP] + (m - 2) * 8 + NS)
                LD.inc(sync.dma_start(out=xv[m % 2][:], in_=sp_x[m]), 16)
                ld_xv[m] = LD.n
            emit_stores(ND - 2)
            emit_stores(ND - 1)
        block.sync(_)

        def _(t):
            t.wait_ge(LD.h, ld_wdt)
            i = 0
            for m in range(ND):
                for tq in range(NS):
                    ps = psA4 if i % 2 == 0 else psB4
                    if i >= 2:
                        j = i - 2
                        t.wait_ge(VP.h, b[VP] + (j // 4) * 8 + (j % 4) + 1)
                    for ic in range(NI):
                        ins = t.matmul(
                            ps[:], wdT[:, ic * D + m * 128:ic * D + (m + 1) * 128],
                            mid[:, ic * S + tq * 512:ic * S + (tq + 1) * 512],
                            start=(ic == 0), stop=(ic == NI - 1))
                    TP.inc(ins, 1)
                    i += 1
                t.wait_ge(VP.h, b[VP] + m * 8 + NS)
                for g in range(4):
                    pst = psT4 if g % 2 == 0 else psT5
                    if g >= 2:
                        t.wait_ge(VP.h, b[VP] + m * 8 + NS + g - 1)
                    for j in range(4):
                        n = g * 4 + j
                        ins = t.transpose(pst[:, j * 128:(j + 1) * 128],
                                          otc[m % 2][:, n * 128:(n + 1) * 128],
                                          identt[:])
                    TP.inc(ins, 1)
        block.tensor(_)

        def _(v):
            i = 0
            for m in range(ND):
                v.wait_ge(LD.h, ld_xv[m])
                for tq in range(NS):
                    ps = psA4 if i % 2 == 0 else psB4
                    v.wait_ge(TP.h, b[TP] + m * 8 + tq + 1)
                    sl = slice(tq * 512, (tq + 1) * 512)
                    VP.inc(v.scalar_tensor_tensor(
                        otc[m % 2][:, sl], ps[:], gbsoP[:, m:m + 1],
                        xv[m % 2][:, sl], ALU.mult, ALU.add), 1)
                    i += 1
                for g in range(4):
                    pst = psT4 if g % 2 == 0 else psT5
                    v.wait_ge(TP.h, b[TP] + m * 8 + NS + g + 1)
                    if m >= 2:
                        v.wait_ge(LD.h, st_ld[(m - 2, g)])
                    VP.inc(v.tensor_copy(
                        otok[m % 2][:, g * 512:(g + 1) * 512], pst[:]), 1)
        block.vector(_)

        def _(sync):
            sync.wait_ge(LD.h, LD.n)
        block.sync(_)

        def _(g):
            g.wait_ge(LD.h, LD.n)
            if debug:
                g.wait_ge(DS.h, DS.n)
            g.wait_ge(LS.h, LS.n)
            g.wait_ge(LG.h, LG.n)
        block.gpsimd(_)
        ck(99)
      except _Stop:
        pass

    return nc, dbg


def make_consts():
    import ml_dtypes
    ident = np.eye(128, dtype=np.float32)
    ones_row = np.ones((128, 128), dtype=np.float32)
    ones_col = np.ones((128, 1), dtype=np.float32)
    ones_col_bf = np.ones((128, 1), dtype=ml_dtypes.bfloat16)
    t = np.arange(S, dtype=np.float64)
    dra = np.exp(-LAM * t).astype(np.float32)[None, :]
    drb = np.exp(LAM * t).astype(np.float32)[None, :]
    k = np.arange(128, dtype=np.float64)
    km, qm = np.meshgrid(k, k, indexing="ij")
    cmask = np.exp(-2 * LAM * np.maximum(km - qm, 0.0)).astype(np.float32)
    return dict(identc=ident, ones_row=ones_row, ones_col=ones_col,
                ones_col_bf=ones_col_bf, dra=dra, drb=drb, cmask=cmask)


def kernel(**inputs):
    debug = int(os.environ.get("MK_DEBUG", "0"))
    stop = float(os.environ.get("MK_STOP", "99"))
    dh = int(os.environ.get("MK_DUMP_HALF", "0"))
    nc, dbg = build(debug, stop, dh)
    consts = make_consts()
    x = np.ascontiguousarray(inputs["L0_attn_x"], dtype=np.float32)
    c = np.ascontiguousarray(inputs["L0_c"], dtype=np.float32)
    base = dict(
        wmod=np.ascontiguousarray(inputs["L0_wmod"], dtype=np.float32),
        Wq=np.ascontiguousarray(inputs["L0_attn_W_q"], dtype=np.float32),
        Wk=np.ascontiguousarray(inputs["L0_attn_W_k"], dtype=np.float32),
        Wv=np.ascontiguousarray(inputs["L0_attn_W_v"], dtype=np.float32),
        Wo=np.ascontiguousarray(inputs["L0_attn_W_o"], dtype=np.float32),
        Wdkv=np.ascontiguousarray(inputs["L0_attn_W_down_kv"], dtype=np.float32),
        Wupk=np.ascontiguousarray(inputs["L0_attn_W_up_k"], dtype=np.float32),
        Wupv=np.ascontiguousarray(inputs["L0_attn_W_up_v"], dtype=np.float32),
        w_gate=np.ascontiguousarray(inputs["L0_ffn_w_gate"], dtype=np.float32),
        w_up=np.ascontiguousarray(inputs["L0_ffn_w_up"], dtype=np.float32),
        w_down=np.ascontiguousarray(inputs["L0_ffn_w_down"], dtype=np.float32),
    )
    base.update(consts)
    in_maps = []
    for bb in range(8):
        m = dict(base)
        m["x"] = np.ascontiguousarray(x[bb])
        m["c"] = np.ascontiguousarray(c[bb:bb + 1])
        in_maps.append(m)
    res = run_bass_kernel_spmd(nc, in_maps, core_ids=list(range(8)))
    out = np.stack([np.asarray(r["y"]) for r in res.results], axis=0)
    if debug:
        kernel.dbg_results = res.results
    return out
